# version 08
# baseline (speedup 1.0000x reference)
import numpy as np
import ml_dtypes
from contextlib import ExitStack
import concourse.bass as bass
import concourse.mybir as mybir
from concourse.bass_utils import run_bass_kernel_spmd

F32 = mybir.dt.float32
BF16 = mybir.dt.bfloat16
AF = mybir.ActivationFunctionType
ALU = mybir.AluOpType
AX = mybir.AxisListType
bf16 = ml_dtypes.bfloat16

B_, SEQ, D = 2, 8192, 2048
NH, DH = 16, 128
HF = 8192
NQT = 16
NSC = 4
TOK = NQT * 128
EPS = 1e-5
NEG = -3.0e38
DBG = False


class Bld:

    def __init__(self, nc):
        self.nc = nc
        self.eng = {"pe": nc.tensor, "act": nc.scalar, "dve": nc.vector,
                    "pool": nc.gpsimd, "sp": nc.sync}
        self.sem = {}
        self.cnt = {}
        self.wm = {}
        self.lw = {}
        self.rd = {}
        self.pend = {k: [] for k in self.eng}
        self.n_ops = 0

    def add_sem(self, name, handle):
        self.sem[name] = handle
        self.cnt[name] = 0

    def _flush(self, deng):
        sname = deng
        ins = self.eng[deng].nop()
        ins.then_inc(self.sem[sname], 1)
        self.cnt[sname] += 1
        for m in self.pend[deng]:
            m[1] = sname
            m[2] = self.cnt[sname]
        self.pend[deng] = []

    SELF_EDGE = {"dve", "act", "pool"}

    def _wait(self, ename, dep, war=False):
        if dep[0] == ename and (war or ename not in self.SELF_EDGE):
            return
        if dep[2] is None:
            self._flush(dep[0])
        key = (ename, dep[1])
        if self.wm.get(key, 0) >= dep[2]:
            return
        self.eng[ename].wait_ge(self.sem[dep[1]], dep[2])
        self.wm[key] = dep[2]

    def op(self, ename, fn, reads=(), writes=(), sem=None, inc=1, dma=False):
        if sem is None:
            sem = ename
        for k in reads:
            d = self.lw.get(k)
            if d is not None:
                self._wait(ename, d)
        for k in writes:
            d = self.lw.get(k)
            if d is not None:
                self._wait(ename, d)
            for d in self.rd.get(k, ()):
                self._wait(ename, d, war=True)
        instr = fn(self.eng[ename])
        self.n_ops += 1
        if inc:
            amt = 16 if dma else inc
            instr.then_inc(self.sem[sem], amt)
            self.cnt[sem] += amt
            mark = [ename, sem, self.cnt[sem]]
            for m in self.pend[ename]:
                m[1] = sem
                m[2] = self.cnt[sem]
            self.pend[ename] = []
        else:
            mark = [ename, sem, None]
            self.pend[ename].append(mark)
        for k in reads:
            self.rd.setdefault(k, []).append(mark)
        for k in writes:
            self.lw[k] = mark
            self.rd[k] = []
        return instr

    def barrier(self):
        for deng in self.eng:
            if self.pend[deng]:
                self._flush(deng)
        for ename in self.eng:
            for s, c in self.cnt.items():
                if c and self.wm.get((ename, s), 0) < c:
                    self.eng[ename].wait_ge(self.sem[s], c)
                    self.wm[(ename, s)] = c


def build_nc(debug=False):
    nc = bass.Bass("TRN2", target_bir_lowering=False)

    dram = {}
    def din(name, shape, dt):
        dram[name] = nc.dram_tensor(name, shape, dt, kind="ExternalInput").ap()
    def dout(name, shape, dt):
        dram[name] = nc.dram_tensor(name, shape, dt, kind="ExternalOutput").ap()

    din("xq", [TOK, D], F32)
    din("xkv", [SEQ, D], F32)
    din("wq_t", [NH, 128, 16, 128], F32)
    din("wo_t", [16, 128, 16, 128], F32)
    din("wkv", [128, 16, 256], F32)
    din("wgu", [2 * HF, D], F32)
    din("wpr", [D, HF], F32)
    din("g16", [128, 16], F32)
    din("cosk", [128, SEQ], BF16)
    din("sink", [128, SEQ], BF16)
    din("cosq", [128, TOK], BF16)
    din("sinq", [128, TOK], BF16)
    din("maskB", [128, 512], BF16)
    din("ident", [128, 128], BF16)
    dout("OUT", [TOK, D], F32)
    if debug:
        dout("DBG_K", [128, SEQ], F32)
        dout("DBG_V", [128, 64 * 129], F32)
        dout("DBG_Q", [128, NH * 512], F32)
        dout("DBG_O", [128, NH * 512], F32)
        dout("DBG_AR", [128, 4 * D], F32)
        dout("DBG_X2", [128, 4 * D], F32)
        dout("DBG_H", [128, 2048], F32)
        dout("DBG_HF", [128, 64 * 512], BF16)
        dout("DBG_FFT", [128, 4 * D], BF16)
        dout("DBG_OB", [128, D], F32)
        dout("DBG_T2", [128, D], F32)
        dout("DBG_CC", [128, 12], F32)
        dout("DBG_ART0", [128, D], BF16)
        dout("DBG_WH", [128, D], BF16)
        dout("DBG_WP", [128, HF], BF16)
        dout("DBG_GS", [128, 2048], BF16)
        dout("DBG_US", [128, 2048], BF16)

    dram["sguT"] = nc.dram_tensor("sguT", [128, 128, 2048], BF16, kind="Internal").ap()
    dram["sprT"] = nc.dram_tensor("sprT", [16, 128, 8192], BF16, kind="Internal").ap()

    es = ExitStack()
    sb = {}
    def S(name, shape, dt):
        sb[name] = es.enter_context(nc.sbuf_tensor(name, shape, dt))
        return sb[name]

    with es:
        S("Ks", [128, SEQ], BF16)
        S("Vs", [128, 64, 129], BF16)
        S("xres", [128, 4 * D], F32)
        S("xnfm", [128, 16, 512], BF16)
        S("shr", [128, 8192], BF16)
        S("hslab", [128, 64, 512], BF16)
        S("g16b", [128, 16], F32)
        S("rotb", [128, 512], BF16)
        S("cqb", [128, 512], BF16)
        S("sqb", [128, 512], BF16)
        S("wqs", [128, 2, 2048], BF16)
        S("wgub", [128, 2, 16, 128], BF16)
        S("wgu2", [128, 2, 16, 128], BF16)
        S("evac", [128, 2, 512], BF16)
        S("PT0", [128, 512], BF16)
        S("PT1", [128, 512], BF16)
        S("sgx", [128, 512], F32)
        S("sux", [128, 512], F32)
        S("tmpa", [128, 512], F32)
        S("tmpb", [128, 512], F32)
        S("tmpr", [128, 512], F32)
        S("mskB", [128, 512], BF16)
        S("idn", [128, 128], BF16)
        S("onesr", [1, 128], BF16)
        S("chalf", [128, 2], F32)
        S("stats", [128, 8], F32)
        S("bnbuf", [128, 24], F32)
        S("negmb", [128, 24], F32)
        S("ccol", [128, 12], F32)
        S("negsc", [1, 2], F32)
        S("knb", [1, 24], F32)
        S("onesc", [128, 1], BF16)
        S("cb16", [128, 16], F32)
        S("negsb", [128, 1], BF16)
        S("negc1", [1, 1], BF16)
        S("sbfT", [128, 64], BF16)
        S("qsum", [128, 64], F32)
        S("shlfT", [128, 64], F32)
        ps = [es.enter_context(nc.psum_tensor("ps%d" % i, [128, 512], F32))
              for i in range(8)]

        b = Bld(nc)
        for name in ["pe", "act", "dve", "pool", "sp"]:
            b.add_sem(name, es.enter_context(nc.semaphore("s_" + name)))

        emit(nc, b, dram, sb, ps, debug)
        print("emitted ops:", b.n_ops)
    return nc


def emit(nc, b, dram, sb, ps, debug):
    K = sb["Ks"]; V = sb["Vs"]
    hv = sb["hslab"].ap().rearrange("p a c -> p (a c)")
    hvf = hv.bitcast(F32)
    shr = sb["shr"].ap()
    Q = shr.rearrange("p (h c) -> p h c", c=512)
    art = shr.rearrange("p (t d) -> p t d", d=D)
    xnfm = sb["xnfm"]

    b.op("sp", lambda e: e.dma_start(out=sb["idn"][:], in_=dram["ident"]),
         writes=["idn"], dma=True)
    b.op("sp", lambda e: e.dma_start(out=sb["mskB"][:], in_=dram["maskB"]),
         writes=["mskB"], dma=True)
    b.op("sp", lambda e: e.dma_start(out=sb["g16b"][:], in_=dram["g16"]),
         writes=["g16"], dma=True)
    b.op("dve", lambda e: e.memset(sb["onesr"][:], 1.0), writes=["onesr"])
    b.op("dve", lambda e: e.memset(sb["onesc"][:], 1.0), writes=["onesc"])
    b.op("dve", lambda e: e.memset(sb["chalf"][:, 0:1], -0.502), writes=["chalf"])
    b.op("dve", lambda e: e.memset(sb["chalf"][:, 1:2], 0.502), writes=["chalf"])
    b.op("dve", lambda e: e.memset(V[:, :, 128:129], 1.0), writes=["Vones"])

    qwt = [hv[:, 20480:22528], hv[:, 22528:24576]]
    qzb = [hv[:, 24576:26624], hv[:, 26624:28672]]
    qab = [hv[:, 28672:30720], hv[:, 30720:32768]]
    qstg = [sb["wqs"][:, 0, :], sb["wqs"][:, 1, :]]
    qF64 = 32

    def quant_tile(qi):
        if qi < 128:
            wsrc = dram["wgu"][qi * 128:(qi + 1) * 128, :]
            sdst = dram["sguT"][qi]
        else:
            r, c = divmod(qi - 128, 4)
            wsrc = dram["wpr"][r * 128:(r + 1) * 128, c * 2048:(c + 1) * 2048]
            sdst = dram["sprT"][r][:, c * 2048:(c + 1) * 2048]
        slot = qi % 2
        wkey = "qw%d" % slot
        zkey = "qz%d" % slot
        akey = "qa%d" % slot
        skey = "wq%d" % slot
        wt = qwt[slot]; qz = qzb[slot]; qa = qab[slot]; stg = qstg[slot]
        b.op("pool", lambda e, wsrc=wsrc, wt=wt: e.dma_start(
                out=wt, in_=wsrc, max_dma_last_dim=8192),
             reads=["hs"], writes=[wkey], dma=True)
        wv = wt.rearrange("p (g k) -> p g k", k=64)
        b.op("dve", lambda e, wv=wv: e.tensor_reduce(
                sb["qsum"][:, 0:qF64], wv, op=ALU.add, axis=AX.X,
                apply_absolute_value=True),
             reads=[wkey], writes=["gsum"])
        b.op("dve", lambda e: e.tensor_scalar(
                sb["sbfT"][:, 0:qF64], sb["qsum"][:, 0:qF64], 1.0 / 64, 1e-8,
                ALU.mult, ALU.max),
             reads=["gsum"], writes=["sbf"])
        b.op("dve", lambda e: e.reciprocal(
                sb["qsum"][:, 32:32 + qF64], sb["sbfT"][:, 0:qF64]),
             reads=["sbf"], writes=["rs"])
        rsb = sb["qsum"][:, 32:32 + qF64].unsqueeze(2).broadcast_to([128, qF64, 64])
        b.op("pool", lambda e, wv=wv, rsb=rsb, qz=qz: e.tensor_tensor(
                qz.rearrange("p (g k) -> p g k", k=64), wv, rsb, op=ALU.mult),
             reads=[wkey, "rs", "hs"], writes=[zkey])
        b.op("dve", lambda e, qz=qz, qa=qa: e.tensor_scalar(
                qa, qz, 0.502, None, ALU.is_gt, ALU.bypass),
             reads=[zkey, "hs"], writes=[akey])
        b.op("dve", lambda e, qz=qz: e.tensor_scalar(
                qz, qz, -0.502, None, ALU.is_lt, ALU.bypass),
             reads=[zkey, "hs"], writes=[zkey])
        b.op("dve", lambda e, qz=qz, qa=qa: e.tensor_tensor(
                qa, qa, qz, op=ALU.subtract),
             reads=[zkey, akey, "hs"], writes=[akey])
        sfb = sb["sbfT"][:, 0:qF64].unsqueeze(2).broadcast_to([128, qF64, 64])
        b.op("pool", lambda e, qa=qa, qz=qz, sfb=sfb: e.tensor_tensor(
                qz.rearrange("p (g k) -> p g k", k=64),
                qa.rearrange("p (g k) -> p g k", k=64), sfb, op=ALU.mult),
             reads=[akey, "sbf", "hs"], writes=[zkey])
        pq = 3 + slot
        for dc in range(16):
            b.op("pe", lambda e, qz=qz, dc=dc, pq=pq: e.matmul(
                    ps[pq][:, (dc % 4) * 128:(dc % 4 + 1) * 128],
                    qz[:, dc * 128:(dc + 1) * 128], sb["idn"][:],
                    start=True, stop=True),
                 reads=[zkey, "idn"], writes=["ps%d" % pq],
                 inc=(1 if dc % 4 == 3 else 0))
            if dc % 4 == 3:
                dcq = dc // 4
                b.op("act", lambda e, stg=stg, dcq=dcq, pq=pq: e.activation(
                        stg[:, dcq * 512:(dcq + 1) * 512], ps[pq][:, :], AF.Copy),
                     reads=["ps%d" % pq], writes=[skey])
        b.op("sp", lambda e, sdst=sdst, stg=stg: e.dma_start(out=sdst, in_=stg),
             reads=[skey], writes=["scr"], dma=True)

    qstate = [0]

    def emit_quant(n):
        for _ in range(n):
            if qstate[0] < 192:
                quant_tile(qstate[0])
                qstate[0] += 1

    def ln_stats4(items, double):
        n = len(items)
        for j, (key_in, ap_in) in enumerate(items):
            for q in range(4):
                b.op("dve", lambda e, q=q, j=j, ap_in=ap_in: e.bn_stats(
                        sb["bnbuf"][:, 6 * q:6 * q + 6],
                        ap_in[:, 512 * q:512 * q + 512]),
                     reads=[key_in], writes=["bn"], inc=(1 if q == 3 else 0))
            b.op("dve", lambda e, j=j: e.bn_aggr(sb["stats"][:, 2 * j:2 * j + 2],
                                                 sb["bnbuf"][:, 0:24]),
                 reads=["bn"], writes=["mv"])
        mview = sb["stats"].ap().rearrange("p (j two) -> p j two", two=2)
        b.op("dve", lambda e, n=n, mview=mview: e.tensor_scalar(
                sb["ccol"][:, 0:n], mview[:, 0:n, 0:1].rearrange("p j o -> p (j o)"),
                -1.0, None, ALU.mult, ALU.bypass),
             reads=["mv"], writes=["negmu"])
        if double:
            b.op("dve", lambda e, n=n, mview=mview: e.tensor_scalar(
                    sb["ccol"][:, 4:4 + n],
                    mview[:, 0:n, 1:2].rearrange("p j o -> p (j o)"),
                    1.0 + EPS, EPS * EPS, ALU.mult, ALU.add),
                 reads=["mv"], writes=["rstd"])
        else:
            b.op("dve", lambda e, n=n, mview=mview: e.tensor_scalar(
                    sb["ccol"][:, 4:4 + n],
                    mview[:, 0:n, 1:2].rearrange("p j o -> p (j o)"),
                    1.0, EPS, ALU.mult, ALU.add),
                 reads=["mv"], writes=["rstd"])
        b.op("act", lambda e, n=n: e.activation(sb["ccol"][:, 4:4 + n],
                                                sb["ccol"][:, 4:4 + n], AF.Ln),
             reads=["rstd"], writes=["rstd"])
        b.op("act", lambda e, n=n: e.activation(sb["ccol"][:, 4:4 + n],
                                                sb["ccol"][:, 4:4 + n],
                                                AF.Exp, scale=-0.5),
             reads=["rstd"], writes=["rstd"])

    def rope(ps_in, pskey, cos_ap, cos_keys, sin_ap, dst_ap, dstkey, n):
        b.op("dve", lambda e: e.tensor_copy(sb["rotb"][0:64, 0:n],
                                            ps_in[64:128, 0:n]),
             reads=[pskey], writes=["rotb"], inc=0)
        b.op("dve", lambda e: e.tensor_copy(sb["rotb"][64:128, 0:n],
                                            ps_in[0:64, 0:n]),
             reads=[pskey], writes=["rotb"])
        b.op("dve", lambda e: e.tensor_tensor(sb["tmpa"][:, 0:n], ps_in[:, 0:n],
                                              cos_ap, op=ALU.mult),
             reads=[pskey] + cos_keys, writes=["tmpa"])
        b.op("dve", lambda e: e.tensor_tensor(
                sb["tmpb"][:, 0:n], sb["rotb"][:, 0:n],
                sin_ap, op=ALU.mult),
             reads=["rotb"] + cos_keys, writes=["tmpb"])
        b.op("dve", lambda e: e.tensor_tensor(dst_ap, sb["tmpa"][:, 0:n],
                                              sb["tmpb"][:, 0:n], op=ALU.add),
             reads=["tmpa", "tmpb"], writes=[dstkey])

    def ln_apply(xa, xkey, xnt, j):
        b.op("dve", lambda e, j=j: e.tensor_scalar(
                xnt, xa, sb["ccol"][:, j:j + 1], sb["ccol"][:, 4 + j:5 + j],
                ALU.add, ALU.mult),
             reads=[xkey, "negmu", "rstd"], writes=["shr"])

    def transpose_group(gamma):
        for dc in range(16):
            for tt in range(4):
                xs = shr[:, tt * D + dc * 128: tt * D + dc * 128 + 128]
                b.op("pe", lambda e, xs=xs, tt=tt: e.matmul(
                        ps[0][:, tt * 128:(tt + 1) * 128], xs, sb["idn"][:],
                        start=True, stop=True),
                     reads=["shr", "idn"], writes=["ps0"],
                     inc=(1 if tt == 3 else 0))
            if gamma:
                b.op("dve", lambda e, dc=dc: e.tensor_scalar(
                        xnfm[:, dc, :], ps[0][:, :], sb["g16b"][:, dc:dc + 1],
                        None, ALU.mult, ALU.bypass),
                     reads=["ps0", "g16"], writes=["xnfm"])
            else:
                b.op("dve", lambda e, dc=dc: e.tensor_copy(
                        xnfm[:, dc, :], ps[0][:, :]),
                     reads=["ps0"], writes=["xnfm"])

    csk = hv[:, 0:8192]
    snk = hv[:, 8192:16384]
    b.op("sp", lambda e: e.dma_start(out=csk, in_=dram["cosk"]),
         writes=["csk"], dma=True)
    b.op("sp", lambda e: e.dma_start(out=snk, in_=dram["sink"]),
         writes=["snk"], dma=True)
    wkv3 = hv[:, 16384:20480].rearrange("p (a c) -> p a c", c=256)
    b.op("pool", lambda e: e.dma_start(out=wkv3, in_=dram["wkv"]),
         writes=["wkv"], dma=True)

    def p2_loads(g):
        for tt in range(4):
            kt = 4 * g + tt
            xa = sb["xres"][:, tt * D:(tt + 1) * D]
            b.op("act", lambda e, kt=kt, xa=xa: e.dma_start(
                    out=xa, in_=dram["xkv"][kt * 128:(kt + 1) * 128, :]),
                 writes=["x%d" % tt], dma=True)

    p2_loads(0)
    for g in range(16):
        items = [("x%d" % tt, sb["xres"][:, tt * D:(tt + 1) * D])
                 for tt in range(4)]
        ln_stats4(items, double=True)
        for tt in range(4):
            ln_apply(items[tt][1], items[tt][0], shr[:, tt * D:(tt + 1) * D], tt)
        transpose_group(gamma=True)
        if g < 15:
            p2_loads(g + 1)
        for dc in range(16):
            b.op("pe", lambda e, dc=dc: e.matmul(
                    ps[1][:, :], wkv3[:, dc, 0:128], xnfm[:, dc, :],
                    start=(dc == 0), stop=(dc == 15)),
                 reads=["wkv", "xnfm"], writes=["ps1"], inc=(1 if dc == 15 else 0))
        rope(ps[1], "ps1", csk[:, g * 512:(g + 1) * 512], ["csk"],
             snk[:, g * 512:(g + 1) * 512],
             K[:, g * 512:(g + 1) * 512], "K", 512)
        for tt in range(4):
            kt = 4 * g + tt
            for dc in range(16):
                b.op("pe", lambda e, dc=dc, tt=tt: e.matmul(
                        ps[2][:, 0:128], xnfm[:, dc, tt * 128:(tt + 1) * 128],
                        wkv3[:, dc, 128:256], start=(dc == 0), stop=(dc == 15)),
                     reads=["xnfm", "wkv"], writes=["ps2"],
                     inc=(1 if dc == 15 else 0))
            b.op("dve", lambda e, kt=kt: e.tensor_copy(V[:, kt, 0:128],
                                                       ps[2][:, 0:128]),
                 reads=["ps2"], writes=["V"])
        emit_quant(12)

    for kb in range(16):
        b.op("act", lambda e, kb=kb: e.activation(
                sb["rotb"][:], K[:, kb * 512:(kb + 1) * 512], AF.Square),
             reads=["K"], writes=["rotb"])
        b.op("pe", lambda e: e.matmul(ps[7][0:1, 0:512], sb["onesc"][:],
                                      sb["rotb"][:], start=True, stop=True),
             reads=["rotb", "onesc"], writes=["ps7"])
        b.op("dve", lambda e, kb=kb: e.tensor_reduce(
                sb["knb"][0:1, kb:kb + 1], ps[7][0:1, 0:512], op=ALU.max, axis=AX.X),
             reads=["ps7"], writes=["knb"])
    b.op("dve", lambda e: e.tensor_reduce(
            sb["knb"][0:1, 16:17], sb["knb"][0:1, 0:16], op=ALU.max, axis=AX.X),
         reads=["knb"], writes=["k2m"])

    emit_quant(192)

    if debug:
        b.op("pool", lambda e: e.dma_start(out=dram["DBG_K"], in_=K[:]),
             reads=["K"], dma=True)
        b.op("pool", lambda e: e.dma_start(
                out=dram["DBG_V"], in_=V.ap().rearrange("p t c -> p (t c)")),
             reads=["V", "Vones"], dma=True)
    b.barrier()

    for s in range(NSC):
        b.op("sp", lambda e, s=s: e.dma_start(
                out=sb["cqb"][:], in_=dram["cosq"][:, s * 512:(s + 1) * 512]),
             writes=["cq"], dma=True)
        b.op("sp", lambda e, s=s: e.dma_start(
                out=sb["sqb"][:], in_=dram["sinq"][:, s * 512:(s + 1) * 512]),
             writes=["cq"], dma=True)

        items = []
        for tt in range(4):
            xa = sb["xres"][:, tt * D:(tt + 1) * D]
            b.op("act", lambda e, s=s, tt=tt, xa=xa: e.dma_start(
                    out=xa,
                    in_=dram["xq"][(4 * s + tt) * 128:(4 * s + tt + 1) * 128, :]),
                 writes=["x%d" % tt], dma=True)
            items.append(("x%d" % tt, xa))
        ln_stats4(items, double=True)
        for tt in range(4):
            ln_apply(items[tt][1], items[tt][0], shr[:, tt * D:(tt + 1) * D], tt)
        transpose_group(gamma=True)

        for h in range(NH):
            slot = h % 2
            wq = sb["wqs"][:, slot, :].rearrange("p (a c) -> p a c", c=128)
            b.op("pool", lambda e, h=h, wq=wq: e.dma_start(
                    out=wq, in_=dram["wq_t"][h]),
                 writes=["wq%d" % slot], dma=True)
            for dc in range(16):
                b.op("pe", lambda e, dc=dc, wq=wq: e.matmul(
                        ps[6][:, :], wq[:, dc, :], xnfm[:, dc, :],
                        start=(dc == 0), stop=(dc == 15)),
                     reads=["wq%d" % slot, "xnfm"], writes=["ps6"],
                     inc=(1 if dc == 15 else 0))
            rope(ps[6], "ps6", sb["cqb"][:], ["cq"], sb["sqb"][:],
                 Q[:, h, :], "shr", 512)
            b.op("act", lambda e, h=h: e.activation(
                    sb["rotb"][:], Q[:, h, :], AF.Square),
                 reads=["shr"], writes=["rotb"])
            b.op("pe", lambda e: e.matmul(ps[7][0:1, 0:512], sb["onesc"][:],
                                          sb["rotb"][:], start=True, stop=True),
                 reads=["rotb", "onesc"], writes=["ps7"])
            b.op("dve", lambda e: e.tensor_reduce(
                    sb["negsc"][0:1, 0:1], ps[7][0:1, 0:512], op=ALU.max, axis=AX.X),
                 reads=["ps7"], writes=["negsc"])
            b.op("dve", lambda e: e.tensor_scalar(
                    sb["negsc"][0:1, 1:2], sb["negsc"][0:1, 0:1],
                    sb["knb"][0:1, 16:17], None, ALU.mult, ALU.bypass),
                 reads=["negsc", "k2m"], writes=["negsc"])
            b.op("act", lambda e: e.activation(sb["negsc"][0:1, 1:2],
                                               sb["negsc"][0:1, 1:2], AF.Ln),
                 reads=["negsc"], writes=["negsc"])
            b.op("act", lambda e: e.activation(sb["negsc"][0:1, 1:2],
                                               sb["negsc"][0:1, 1:2],
                                               AF.Exp, scale=0.5),
                 reads=["negsc"], writes=["negsc"])
            b.op("dve", lambda e: e.tensor_copy(sb["negc1"][0:1, 0:1],
                                                sb["negsc"][0:1, 1:2]),
                 reads=["negsc"], writes=["negc1"])
            b.op("pe", lambda e: e.matmul(ps[7][:, 128:129], sb["onesr"][:],
                                          sb["negc1"][0:1, 0:1],
                                          start=True, stop=True),
                 reads=["negc1", "onesr"], writes=["ps7b"])
            b.op("dve", lambda e, h=h: e.tensor_scalar(
                    sb["cb16"][:, h:h + 1], ps[7][:, 128:129], -1.0, None,
                    ALU.mult, ALU.bypass),
                 reads=["ps7b"], writes=["cb16"])

        if debug and s == 0:
            b.op("pool", lambda e: e.dma_start(out=dram["DBG_Q"], in_=shr),
                 reads=["shr"], dma=True)

        for h in range(NH):
            nkb = 4 * s + 4
            for kb in range(nkb):
                j = max(0, kb - 4 * s)
                q0 = 128 * j
                tail = 512 - q0
                for kk in range(4):
                    pp = (kb * 4 + kk) % 2
                    pskey = "ps%d" % pp
                    b.op("pe", lambda e, kb=kb, kk=kk, q0=q0, tail=tail, pp=pp, h=h:
                         e.matmul(ps[pp][:, 0:tail],
                                  K[:, kb * 512 + kk * 128: kb * 512 + kk * 128 + 128],
                                  Q[:, h, q0:512], start=True, stop=True),
                         reads=["K", "shr"], writes=[pskey])
                    if kb >= 4 * s:
                        b.op("dve", lambda e, kk=kk, pp=pp: e.tensor_tensor(
                                ps[pp][:, 0:128], ps[pp][:, 0:128],
                                sb["mskB"][:, kk * 128:(kk + 1) * 128], op=ALU.add),
                             reads=[pskey, "mskB"], writes=[pskey])
                    ptb = sb["PT0"] if pp == 0 else sb["PT1"]
                    ptkey = "PT%d" % pp
                    b.op("act", lambda e, ptb=ptb, tail=tail, pp=pp, h=h: e.activation(
                            ptb[:, 0:tail], ps[pp][:, 0:tail], AF.Exp,
                            bias=sb["cb16"][:, h:h + 1]),
                         reads=[pskey, "cb16"], writes=[ptkey])
                    for tq in range(j, 4):
                        t = 4 * s + tq
                        b.op("pe", lambda e, tq=tq, q0=q0, kb=kb, kk=kk, ptb=ptb:
                             e.matmul(ps[2 + tq][:, 0:129],
                                      ptb[:, tq * 128 - q0: tq * 128 - q0 + 128],
                                      V[:, kb * 4 + kk, :],
                                      start=(kb == 0 and kk == 0),
                                      stop=(kb == t and kk == 3)),
                             reads=[ptkey, "V", "Vones"], writes=["psO%d" % tq],
                             inc=(1 if (kb == t and kk == 3) else 0))
            for tq in range(4):
                b.op("dve", lambda e, tq=tq: e.reciprocal(
                        sb["ccol"][:, 9:10], ps[2 + tq][:, 128:129]),
                     reads=["psO%d" % tq], writes=["cc2"])
                b.op("dve", lambda e, tq=tq: e.tensor_scalar(
                        sb["rotb"][:, 0:128], ps[2 + tq][:, 0:128],
                        sb["ccol"][:, 9:10], None, ALU.mult, ALU.bypass),
                     reads=["psO%d" % tq, "cc2"], writes=["rotb"])
                b.op("pe", lambda e, tq=tq: e.matmul(
                        ps[6][:, tq * 128:(tq + 1) * 128],
                        sb["rotb"][:, 0:128], sb["idn"][:],
                        start=True, stop=True),
                     reads=["rotb", "idn"], writes=["ps6"],
                     inc=(1 if tq == 3 else 0))
            b.op("act", lambda e, h=h: e.activation(xnfm[:, h, :], ps[6][:, :],
                                                    AF.Copy),
                 reads=["ps6"], writes=["xnfm"])

        if debug and s == 0:
            b.op("pool", lambda e: e.dma_start(
                    out=dram["DBG_O"], in_=xnfm.ap().rearrange("p h c -> p (h c)")),
                 reads=["xnfm"], dma=True)

        for oc in range(16):
            slot = oc % 2
            wo = sb["wqs"][:, slot, :].rearrange("p (a c) -> p a c", c=128)
            b.op("pool", lambda e, oc=oc, wo=wo: e.dma_start(
                    out=wo, in_=dram["wo_t"][oc]),
                 writes=["wq%d" % slot], dma=True)
            for fc in range(16):
                b.op("pe", lambda e, fc=fc, wo=wo: e.matmul(
                        ps[0][:, :], wo[:, fc, :], xnfm[:, fc, :],
                        start=(fc == 0), stop=(fc == 15)),
                     reads=["wq%d" % slot, "xnfm"], writes=["ps0"],
                     inc=(1 if fc == 15 else 0))
            ev = sb["evac"][:, oc % 2, :]
            evk = "ev%d" % (oc % 2)
            b.op("act", lambda e, ev=ev: e.activation(ev, ps[0][:, :], AF.Copy),
                 reads=["ps0"], writes=[evk])
            for tt in range(4):
                b.op("pe", lambda e, ev=ev, tt=tt: e.matmul(
                        ps[1][:, tt * 128:(tt + 1) * 128],
                        ev[:, tt * 128:(tt + 1) * 128], sb["idn"][:],
                        start=True, stop=True),
                     reads=[evk, "idn"], writes=["ps1"], inc=(1 if tt == 3 else 0))
            dst = art[:, :, oc * 128:(oc + 1) * 128]
            b.op("dve", lambda e, dst=dst: e.tensor_copy(
                    dst, ps[1].ap().rearrange("p (t c) -> p t c", c=128)),
                 reads=["ps1"], writes=["shr"])
        if debug and s == 0:
            b.op("pool", lambda e: e.dma_start(out=dram["DBG_AR"], in_=shr),
                 reads=["shr"], dma=True)

        t2f = hvf[:, 0:2048]
        ln_stats4([("shr", art[:, tt, :]) for tt in range(4)], double=False)
        for tt in range(4):
            a_tt = art[:, tt, :]
            xa = sb["xres"][:, tt * D:(tt + 1) * D]
            b.op("dve", lambda e, a_tt=a_tt, t2f=t2f, tt=tt: e.tensor_scalar(
                    t2f, a_tt, sb["ccol"][:, tt:tt + 1], sb["ccol"][:, 4 + tt:5 + tt],
                    ALU.add, ALU.mult),
                 reads=["shr", "negmu", "rstd"], writes=["hs"])
            b.op("pool", lambda e, xa=xa, t2f=t2f: e.tensor_tensor(
                    xa, t2f, xa, op=ALU.add),
                 reads=["hs", "x%d" % tt], writes=["x%d" % tt])
        if debug and s == 0:
            b.op("sp", lambda e: e.dma_start(out=dram["DBG_X2"], in_=sb["xres"][:]),
                 reads=["x0", "x1", "x2", "x3"], dma=True)

        ln_stats4([("x%d" % tt, sb["xres"][:, tt * D:(tt + 1) * D])
                   for tt in range(4)], double=False)
        for tt in range(4):
            ln_apply(sb["xres"][:, tt * D:(tt + 1) * D], "x%d" % tt,
                     shr[:, tt * D:(tt + 1) * D], tt)
        transpose_group(gamma=False)

        for g in range(64):
            slot = g % 2
            gk, uk = "wgA%d" % slot, "wgB%d" % slot
            b.op("sp", lambda e, g=g, slot=slot: e.dma_start(
                    out=sb["wgub"][:, slot, :, :].rearrange("p a c -> p (a c)"),
                    in_=dram["sguT"][g]),
                 reads=["scr"], writes=[gk, "wp0"], dma=True)
            b.op("sp", lambda e, g=g, slot=slot: e.dma_start(
                    out=sb["wgu2"][:, slot, :, :].rearrange("p a c -> p (a c)"),
                    in_=dram["sguT"][64 + g]),
                 reads=["scr"], writes=[uk, "wp1"], dma=True)
            if debug and s == 0 and g == 7:
                b.op("pool", lambda e, slot=slot: e.dma_start(
                        out=dram["DBG_GS"],
                        in_=sb["wgub"][:, slot, :, :].rearrange("p a c -> p (a c)")),
                     reads=[gk], dma=True)
                b.op("pool", lambda e, slot=slot: e.dma_start(
                        out=dram["DBG_US"],
                        in_=sb["wgu2"][:, slot, :, :].rearrange("p a c -> p (a c)")),
                     reads=[uk], dma=True)
            for dc in range(16):
                b.op("pe", lambda e, slot=slot, dc=dc: e.matmul(
                        ps[0][:, :], sb["wgub"][:, slot, dc, :], xnfm[:, dc, :],
                        start=(dc == 0), stop=(dc == 15)),
                     reads=[gk, "xnfm"], writes=["ps0"], inc=(1 if dc == 15 else 0))
            for dc in range(16):
                b.op("pe", lambda e, slot=slot, dc=dc: e.matmul(
                        ps[1][:, :], sb["wgu2"][:, slot, dc, :], xnfm[:, dc, :],
                        start=(dc == 0), stop=(dc == 15)),
                     reads=[uk, "xnfm"], writes=["ps1"], inc=(1 if dc == 15 else 0))
            b.op("act", lambda e: e.activation(sb["sgx"][:], ps[0][:, :], AF.Silu),
                 reads=["ps0"], writes=["sgx"])
            b.op("dve", lambda e: e.tensor_copy(sb["sux"][:], ps[1][:, :]),
                 reads=["ps1"], writes=["sux"])
            b.op("dve", lambda e, g=g: e.tensor_tensor(
                    sb["hslab"][:, g, :], sb["sgx"][:], sb["sux"][:], op=ALU.mult),
                 reads=["sgx", "sux"], writes=["hs"])
        if debug and s == 0:
            b.op("pool", lambda e: e.dma_start(
                    out=dram["DBG_H"], in_=hv[:, 0:2048]),
                 reads=["hs"], dma=True)

        for dc in range(16):
            for hh in range(2):
                wpt = sb["wgub"] if hh == 0 else sb["wgu2"]
                wpk = "wgA0" if hh == 0 else "wgB0"
                wp3 = wpt.ap().rearrange("p s a c -> p (s a) c")
                b.op("sp", lambda e, dc=dc, hh=hh, wp3=wp3: e.dma_start(
                        out=wp3.rearrange("p a c -> p (a c)"),
                        in_=dram["sprT"][dc][:, hh * 4096:(hh + 1) * 4096]),
                     reads=["scr"], writes=[wpk, "wgA1" if hh == 0 else "wgB1"],
                     dma=True)
                for hc in range(32):
                    b.op("pe", lambda e, hc=hc, hh=hh, wp3=wp3, dc=dc: e.matmul(
                            ps[0][:, :], wp3[:, hc, :],
                            sb["hslab"][:, hh * 32 + hc, :],
                            start=(hh == 0 and hc == 0), stop=(hh == 1 and hc == 31)),
                         reads=[wpk, "hs"], writes=["ps0"],
                         inc=(1 if (hh == 1 and hc == 31) else 0))
            ev = sb["evac"][:, dc % 2, :]
            evk = "ev%d" % (dc % 2)
            b.op("act", lambda e, ev=ev: e.activation(ev, ps[0][:, :], AF.Copy),
                 reads=["ps0"], writes=[evk])
            for tt in range(4):
                b.op("pe", lambda e, ev=ev, tt=tt: e.matmul(
                        ps[1][:, tt * 128:(tt + 1) * 128],
                        ev[:, tt * 128:(tt + 1) * 128], sb["idn"][:],
                        start=True, stop=True),
                     reads=[evk, "idn"], writes=["ps1"], inc=(1 if tt == 3 else 0))
            dst = art[:, :, dc * 128:(dc + 1) * 128]
            b.op("dve", lambda e, dst=dst: e.tensor_copy(
                    dst, ps[1].ap().rearrange("p (t c) -> p t c", c=128)),
                 reads=["ps1"], writes=["shr"])

        if debug and s == 0:
            b.op("pool", lambda e: e.dma_start(
                    out=dram["DBG_FFT"], in_=shr), reads=["shr"], dma=True)
            b.op("pool", lambda e: e.dma_start(
                    out=dram["DBG_HF"], in_=hv[:, 0:32768]), reads=["hs"], dma=True)
        ln_stats4([("shr", art[:, tt, :]) for tt in range(4)], double=False)
        for tt in range(4):
            a_tt = art[:, tt, :]
            xa = sb["xres"][:, tt * D:(tt + 1) * D]
            ob = hvf[:, 2048 + (tt % 2) * 2048: 2048 + (tt % 2 + 1) * 2048]
            obk = "ob%d" % (tt % 2)
            b.op("dve", lambda e, a_tt=a_tt, t2f=t2f, tt=tt: e.tensor_scalar(
                    t2f, a_tt, sb["ccol"][:, tt:tt + 1], sb["ccol"][:, 4 + tt:5 + tt],
                    ALU.add, ALU.mult),
                 reads=["shr", "negmu", "rstd"], writes=["hs"])
            b.op("pool", lambda e, xa=xa, ob=ob, t2f=t2f: e.tensor_tensor(
                    ob, t2f, xa, op=ALU.add),
                 reads=["hs", "x%d" % tt], writes=[obk])
            if debug and s == 0 and tt == 0:
                b.op("sp", lambda e, ob=ob: e.dma_start(out=dram["DBG_OB"], in_=ob),
                     reads=[obk, "hs"], dma=True)
                b.op("sp", lambda e, t2f=t2f: e.dma_start(out=dram["DBG_T2"], in_=t2f),
                     reads=["hs"], dma=True)
                b.op("sp", lambda e: e.dma_start(out=dram["DBG_CC"], in_=sb["ccol"][:]),
                     reads=["negmu", "rstd"], dma=True)
                b.op("pool", lambda e, a_tt=a_tt: e.dma_start(out=dram["DBG_ART0"], in_=a_tt),
                     reads=["shr"], dma=True)
            b.op("act", lambda e, s=s, tt=tt, ob=ob: e.dma_start(
                    out=dram["OUT"][(4 * s + tt) * 128:(4 * s + tt + 1) * 128, :],
                    in_=ob),
                 reads=[obk, "hs"], dma=True)

    if debug:
        b.op("pool", lambda e: e.dma_start(out=dram["DBG_WH"],
                                           in_=dram["sguT"][7].rearrange("p c -> p c")),
             reads=["scr"], dma=True)
    b.barrier()


def _rope_tables():
    inv = 1.0 / 10000.0 ** (np.arange(0, DH, 2, dtype=np.float64) / DH)
    pos = np.arange(SEQ, dtype=np.float64)
    fr = pos[:, None] * inv[None, :]
    fr = np.concatenate([fr, fr], axis=1)
    cos = np.cos(fr).astype(np.float32)
    sin = np.sin(fr).astype(np.float32)
    sgn = np.ones((1, DH), np.float32)
    sgn[0, :64] = -1.0
    return cos.T.copy(), (sin * sgn).T.copy()


def kernel(L0_attn_x, L0_attn_gamma, L0_attn_Wq, L0_attn_Wkv, L0_attn_Wo,
           L0_ffn_w_gate_up, L0_ffn_w_proj):
    x = np.ascontiguousarray(np.asarray(L0_attn_x, dtype=np.float32))
    gamma = np.asarray(L0_attn_gamma, dtype=np.float32)
    Wq = np.asarray(L0_attn_Wq, dtype=np.float32)
    Wkv = np.asarray(L0_attn_Wkv, dtype=np.float32)
    Wo = np.asarray(L0_attn_Wo, dtype=np.float32)
    Wgu = np.ascontiguousarray(np.asarray(L0_ffn_w_gate_up, dtype=np.float32))
    Wpr = np.ascontiguousarray(np.asarray(L0_ffn_w_proj, dtype=np.float32))

    cosk, sink = _rope_tables()
    scale = DH ** -0.5
    wq_t = np.ascontiguousarray(Wq.reshape(16, 128, NH, 128).transpose(2, 1, 0, 3))
    wo_t = np.ascontiguousarray(Wo.reshape(16, 128, 16, 128).transpose(2, 1, 0, 3))
    wkv_t = np.ascontiguousarray(Wkv.reshape(16, 128, 256).transpose(1, 0, 2))
    g16 = np.ascontiguousarray(gamma.reshape(16, 128).T)
    ident = np.eye(128, dtype=np.float32).astype(bf16)
    cosk_b = cosk.astype(bf16)
    sink_b = sink.astype(bf16)

    in_maps = []
    for c in range(8):
        bb, i = divmod(c, 4)
        idx = [4 * t + i for t in range(NQT)]
        xq = np.ascontiguousarray(x[bb].reshape(64, 128, D)[idx].reshape(TOK, D))
        cq = np.ascontiguousarray(
            (cosk * scale).reshape(128, 64, 128)[:, idx, :].reshape(128, TOK)
        ).astype(bf16)
        sq = np.ascontiguousarray(
            (sink * scale).reshape(128, 64, 128)[:, idx, :].reshape(128, TOK)
        ).astype(bf16)
        r = np.arange(128)[:, None]
        cc = np.arange(128)[None, :]
        mB = np.zeros((128, 512), dtype=np.float32)
        for kk in range(4):
            mB[:, kk * 128:(kk + 1) * 128] = np.where(
                (128 * kk + r) <= (128 * i + cc), 0.0, NEG)
        in_maps.append(dict(
            xq=xq, xkv=np.ascontiguousarray(x[bb]), wq_t=wq_t, wo_t=wo_t,
            wkv=wkv_t, wgu=Wgu, wpr=Wpr, g16=g16,
            cosk=cosk_b, sink=sink_b, cosq=cq, sinq=sq,
            maskB=mB.astype(bf16), ident=ident))

    nc = build_nc(debug=DBG)
    res = run_bass_kernel_spmd(nc, in_maps, core_ids=list(range(8)))

    out = np.empty((B_, SEQ, D), dtype=np.float32)
    for c in range(8):
        bb, i = divmod(c, 4)
        rr = np.asarray(res.results[c]["OUT"], dtype=np.float32)
        idx = [4 * t + i for t in range(NQT)]
        out[bb].reshape(64, 128, D)[idx] = rr.reshape(NQT, 128, D)
    return out




# version 15
# speedup vs baseline: 1.3538x; 1.3538x over previous
import numpy as np
import ml_dtypes
from contextlib import ExitStack
import concourse.bass as bass
import concourse.mybir as mybir
from concourse.bass_utils import run_bass_kernel_spmd

F32 = mybir.dt.float32
BF16 = mybir.dt.bfloat16
AF = mybir.ActivationFunctionType
ALU = mybir.AluOpType
AX = mybir.AxisListType
bf16 = ml_dtypes.bfloat16

B_, SEQ, D = 2, 8192, 2048
NH, DH = 16, 128
HF = 8192
NQT = 16
NSC = 4
TOK = NQT * 128
EPS = 1e-5
NEG = -3.0e38
DBG = False


class Bld:

    def __init__(self, nc):
        self.nc = nc
        self.eng = {"pe": nc.tensor, "act": nc.scalar, "dve": nc.vector,
                    "pool": nc.gpsimd, "sp": nc.sync}
        self.sem = {}
        self.cnt = {}
        self.wm = {}
        self.lw = {}
        self.rd = {}
        self.pend = {k: [] for k in self.eng}
        self.n_ops = 0

    def add_sem(self, name, handle):
        self.sem[name] = handle
        self.cnt[name] = 0

    def _flush(self, deng):
        sname = deng
        ins = self.eng[deng].nop()
        ins.then_inc(self.sem[sname], 1)
        self.cnt[sname] += 1
        for m in self.pend[deng]:
            m[1] = sname
            m[2] = self.cnt[sname]
        self.pend[deng] = []

    SELF_EDGE = {"dve", "act", "pool", "sp"}

    def _wait(self, ename, dep, war=False):
        if dep[0] == ename and (war or ename not in self.SELF_EDGE):
            return
        if dep[2] is None:
            self._flush(dep[0])
        key = (ename, dep[1])
        if self.wm.get(key, 0) >= dep[2]:
            return
        self.eng[ename].wait_ge(self.sem[dep[1]], dep[2])
        self.wm[key] = dep[2]

    def op(self, ename, fn, reads=(), writes=(), sem=None, inc=1, dma=False):
        if sem is None:
            sem = ename
        for k in reads:
            d = self.lw.get(k)
            if d is not None:
                self._wait(ename, d)
        for k in writes:
            d = self.lw.get(k)
            if d is not None:
                self._wait(ename, d)
            for d in self.rd.get(k, ()):
                self._wait(ename, d, war=True)
        instr = fn(self.eng[ename])
        self.n_ops += 1
        if inc:
            amt = 16 if dma else inc
            instr.then_inc(self.sem[sem], amt)
            self.cnt[sem] += amt
            mark = [ename, sem, self.cnt[sem]]
            for m in self.pend[ename]:
                m[1] = sem
                m[2] = self.cnt[sem]
            self.pend[ename] = []
        else:
            mark = [ename, sem, None]
            self.pend[ename].append(mark)
        for k in reads:
            self.rd.setdefault(k, []).append(mark)
        for k in writes:
            self.lw[k] = mark
            self.rd[k] = []
        return instr

    def barrier(self):
        for deng in self.eng:
            if self.pend[deng]:
                self._flush(deng)
        for ename in self.eng:
            for s, c in self.cnt.items():
                if c and self.wm.get((ename, s), 0) < c:
                    self.eng[ename].wait_ge(self.sem[s], c)
                    self.wm[(ename, s)] = c


def build_nc(debug=False):
    nc = bass.Bass("TRN2", target_bir_lowering=False)

    dram = {}
    def din(name, shape, dt):
        dram[name] = nc.dram_tensor(name, shape, dt, kind="ExternalInput").ap()
    def dout(name, shape, dt):
        dram[name] = nc.dram_tensor(name, shape, dt, kind="ExternalOutput").ap()

    din("xq", [TOK, D], F32)
    din("xkv", [SEQ, D], F32)
    din("wq_t", [NH, 128, 16, 128], F32)
    din("wo_t", [16, 128, 16, 128], F32)
    din("wkv", [128, 16, 256], F32)
    din("wgu", [2 * HF, D], F32)
    din("wpr", [D, HF], F32)
    din("g16", [128, 16], F32)
    din("cosk", [128, SEQ], BF16)
    din("sink", [128, SEQ], BF16)
    din("cosq", [128, TOK], BF16)
    din("sinq", [128, TOK], BF16)
    din("maskB", [128, 512], BF16)
    din("ident", [128, 128], BF16)
    dout("OUT", [TOK, D], F32)
    if debug:
        dout("DBG_K", [128, SEQ], F32)
        dout("DBG_V", [128, 64 * 129], F32)
        dout("DBG_Q", [128, NH * 512], F32)
        dout("DBG_O", [128, NH * 512], F32)
        dout("DBG_AR", [128, 4 * D], F32)
        dout("DBG_X2", [128, 4 * D], F32)
        dout("DBG_H", [128, 2048], F32)
        dout("DBG_HF", [128, 64 * 512], BF16)
        dout("DBG_FFT", [128, 4 * D], BF16)
        dout("DBG_OB", [128, D], F32)
        dout("DBG_T2", [128, D], F32)
        dout("DBG_CC", [128, 12], F32)
        dout("DBG_ART0", [128, D], BF16)
        dout("DBG_WH", [128, D], BF16)
        dout("DBG_WP", [128, HF], BF16)
        dout("DBG_GS", [128, 2048], BF16)
        dout("DBG_US", [128, 2048], BF16)

    dram["sguT"] = nc.dram_tensor("sguT", [128, 128, 2048], BF16, kind="Internal").ap()
    dram["sprT"] = nc.dram_tensor("sprT", [16, 128, 8192], BF16, kind="Internal").ap()

    es = ExitStack()
    sb = {}
    def S(name, shape, dt):
        sb[name] = es.enter_context(nc.sbuf_tensor(name, shape, dt))
        return sb[name]

    with es:
        S("Ks", [128, SEQ], BF16)
        S("Vs", [128, 64, 129], BF16)
        S("xres", [128, 4 * D], F32)
        S("xnfm", [128, 16, 512], BF16)
        S("shr", [128, 8192], BF16)
        S("hslab", [128, 64, 512], BF16)
        S("g16b", [128, 16], F32)
        S("rotb", [128, 512], BF16)
        S("cqb", [128, 512], BF16)
        S("sqb", [128, 512], BF16)
        S("wqs", [128, 2, 2048], BF16)
        S("wgub", [128, 2, 16, 128], BF16)
        S("wgu2", [128, 2, 16, 128], BF16)
        S("evac", [128, 2, 512], BF16)
        S("PT0", [128, 512], BF16)
        S("PT1", [128, 512], BF16)
        S("PT2", [128, 512], BF16)
        S("sgx", [128, 512], F32)
        S("sux", [128, 512], F32)
        S("tmpa", [128, 512], F32)
        S("tmpb", [128, 512], F32)
        S("tmpr", [128, 512], F32)
        S("mskB", [128, 512], BF16)
        S("idn", [128, 128], BF16)
        S("onesr", [1, 128], BF16)
        S("chalf", [128, 2], F32)
        S("stats", [128, 8], F32)
        S("bnbuf", [128, 24], F32)
        S("negmb", [128, 24], F32)
        S("ccol", [128, 12], F32)
        S("negsc", [1, 2], F32)
        S("knb", [1, 24], F32)
        S("onesc", [128, 1], BF16)
        S("cb16", [128, 16], F32)
        S("negsb", [128, 1], BF16)
        S("negc1", [1, 1], BF16)
        S("sbfT", [128, 64], BF16)
        S("qsum", [128, 64], F32)
        S("shlfT", [128, 64], F32)
        ps = [es.enter_context(nc.psum_tensor("ps%d" % i, [128, 512], F32))
              for i in range(8)]

        b = Bld(nc)
        for name in ["pe", "act", "dve", "pool", "sp"]:
            b.add_sem(name, es.enter_context(nc.semaphore("s_" + name)))

        emit(nc, b, dram, sb, ps, debug)
        print("emitted ops:", b.n_ops)
    return nc


def emit(nc, b, dram, sb, ps, debug):
    K = sb["Ks"]; V = sb["Vs"]
    hv = sb["hslab"].ap().rearrange("p a c -> p (a c)")
    hvf = hv.bitcast(F32)
    shr = sb["shr"].ap()
    Q = shr.rearrange("p (h c) -> p h c", c=512)
    art = shr.rearrange("p (t d) -> p t d", d=D)
    xnfm = sb["xnfm"]

    b.op("sp", lambda e: e.dma_start(out=sb["idn"][:], in_=dram["ident"]),
         writes=["idn"], dma=True)
    b.op("sp", lambda e: e.dma_start(out=sb["mskB"][:], in_=dram["maskB"]),
         writes=["mskB"], dma=True)
    b.op("sp", lambda e: e.dma_start(out=sb["g16b"][:], in_=dram["g16"]),
         writes=["g16"], dma=True)
    b.op("dve", lambda e: e.memset(sb["onesr"][:], 1.0), writes=["onesr"])
    b.op("dve", lambda e: e.memset(sb["onesc"][:], 1.0), writes=["onesc"])
    b.op("dve", lambda e: e.memset(sb["chalf"][:, 0:1], -0.502), writes=["chalf"])
    b.op("dve", lambda e: e.memset(sb["chalf"][:, 1:2], 0.502), writes=["chalf"])
    b.op("dve", lambda e: e.memset(V[:, :, 128:129], 1.0), writes=["Vones"])

    qwt = [hv[:, 20480:22528], hv[:, 22528:24576]]
    qzb = [hv[:, 24576:26624], hv[:, 26624:28672]]
    qab = [hv[:, 28672:30720], hv[:, 30720:32768]]
    qstg = [sb["wqs"][:, 0, :], sb["wqs"][:, 1, :]]
    qF64 = 32

    def _qsrc(qi):
        if qi < 128:
            return (dram["wgu"][qi * 128:(qi + 1) * 128, :], dram["sguT"][qi])
        r, c = divmod(qi - 128, 4)
        return (dram["wpr"][r * 128:(r + 1) * 128, c * 2048:(c + 1) * 2048],
                dram["sprT"][r][:, c * 2048:(c + 1) * 2048])

    def quant_load(qi):
        wsrc, _ = _qsrc(qi)
        slot = qi % 2
        b.op("pool", lambda e, wsrc=wsrc, slot=slot: e.dma_start(
                out=qwt[slot], in_=wsrc, max_dma_last_dim=8192),
             reads=["hs"], writes=["qw%d" % slot], dma=True)

    def quant_compute(qi):
        _, sdst = _qsrc(qi)
        slot = qi % 2
        wkey = "qw%d" % slot
        zkey = "qz%d" % slot
        akey = "qa%d" % slot
        skey = "wq%d" % slot
        wt = qwt[slot]; qz = qzb[slot]; qa = qab[slot]; stg = qstg[slot]
        wv = wt.rearrange("p (g k) -> p g k", k=64)
        b.op("dve", lambda e, wv=wv: e.tensor_reduce(
                sb["qsum"][:, 0:qF64], wv, op=ALU.add, axis=AX.X,
                apply_absolute_value=True),
             reads=[wkey], writes=["gsum"])
        b.op("dve", lambda e: e.tensor_scalar(
                sb["sbfT"][:, 0:qF64], sb["qsum"][:, 0:qF64], 1.0 / 64, 1e-8,
                ALU.mult, ALU.max),
             reads=["gsum"], writes=["sbf"])
        b.op("dve", lambda e: e.reciprocal(
                sb["qsum"][:, 32:32 + qF64], sb["sbfT"][:, 0:qF64]),
             reads=["sbf"], writes=["rs"])
        rsb = sb["qsum"][:, 32:32 + qF64].unsqueeze(2).broadcast_to([128, qF64, 64])
        b.op("pool", lambda e, wv=wv, rsb=rsb, qz=qz: e.tensor_tensor(
                qz.rearrange("p (g k) -> p g k", k=64), wv, rsb, op=ALU.mult),
             reads=[wkey, "rs", "hs"], writes=[zkey])
        b.op("act", lambda e, qz=qz, qa=qa: e.activation(
                qa, qz, AF.Sign, bias=sb["chalf"][:, 0:1]),
             reads=[zkey, "chalf", "hs"], writes=[akey])
        b.op("act", lambda e, qz=qz: e.activation(
                qz, qz, AF.Sign, bias=sb["chalf"][:, 1:2]),
             reads=[zkey, "hs"], writes=[zkey])
        b.op("dve", lambda e, qz=qz, qa=qa: e.tensor_tensor(
                qa, qa, qz, op=ALU.add),
             reads=[zkey, akey, "hs"], writes=[akey])
        b.op("dve", lambda e: e.tensor_scalar(
                sb["shlfT"][:, 0:qF64], sb["sbfT"][:, 0:qF64], 0.5, None,
                ALU.mult, ALU.bypass),
             reads=["sbf"], writes=["shlf"])
        sfb = sb["shlfT"][:, 0:qF64].unsqueeze(2).broadcast_to([128, qF64, 64])
        b.op("dve", lambda e, qa=qa, qz=qz, sfb=sfb: e.tensor_tensor(
                qz.rearrange("p (g k) -> p g k", k=64),
                qa.rearrange("p (g k) -> p g k", k=64), sfb, op=ALU.mult),
             reads=[akey, "shlf", "hs"], writes=[zkey])
        pq = 3 + slot
        for dc in range(16):
            b.op("pe", lambda e, qz=qz, dc=dc, pq=pq: e.matmul(
                    ps[pq][:, (dc % 4) * 128:(dc % 4 + 1) * 128],
                    qz[:, dc * 128:(dc + 1) * 128], sb["idn"][:],
                    start=True, stop=True),
                 reads=[zkey, "idn"], writes=["ps%d" % pq],
                 inc=(1 if dc % 4 == 3 else 0))
            if dc % 4 == 3:
                dcq = dc // 4
                b.op("act", lambda e, stg=stg, dcq=dcq, pq=pq: e.activation(
                        stg[:, dcq * 512:(dcq + 1) * 512], ps[pq][:, :], AF.Copy),
                     reads=["ps%d" % pq], writes=[skey])
        b.op("sp", lambda e, sdst=sdst, stg=stg: e.dma_start(out=sdst, in_=stg),
             reads=[skey], writes=["scr"], dma=True)

    qstate = [0]

    def emit_quant(n):
        for _ in range(n):
            if qstate[0] < 192:
                quant_load(qstate[0])
                quant_compute(qstate[0])
                qstate[0] += 1

    def emit_quant_pipelined():
        for i in range(193):
            if i < 192:
                quant_load(i)
            if i >= 1:
                quant_compute(i - 1)
        qstate[0] = 192

    def ln_stats4(items, double):
        n = len(items)
        for j, (key_in, ap_in) in enumerate(items):
            for q in range(4):
                b.op("dve", lambda e, q=q, j=j, ap_in=ap_in: e.bn_stats(
                        sb["bnbuf"][:, 6 * q:6 * q + 6],
                        ap_in[:, 512 * q:512 * q + 512]),
                     reads=[key_in], writes=["bn"], inc=(1 if q == 3 else 0))
            b.op("dve", lambda e, j=j: e.bn_aggr(sb["stats"][:, 2 * j:2 * j + 2],
                                                 sb["bnbuf"][:, 0:24]),
                 reads=["bn"], writes=["mv"])
        mview = sb["stats"].ap().rearrange("p (j two) -> p j two", two=2)
        b.op("dve", lambda e, n=n, mview=mview: e.tensor_scalar(
                sb["ccol"][:, 0:n], mview[:, 0:n, 0:1].rearrange("p j o -> p (j o)"),
                -1.0, None, ALU.mult, ALU.bypass),
             reads=["mv"], writes=["negmu"])
        if double:
            b.op("dve", lambda e, n=n, mview=mview: e.tensor_scalar(
                    sb["ccol"][:, 4:4 + n],
                    mview[:, 0:n, 1:2].rearrange("p j o -> p (j o)"),
                    1.0 + EPS, EPS * EPS, ALU.mult, ALU.add),
                 reads=["mv"], writes=["rstd"])
        else:
            b.op("dve", lambda e, n=n, mview=mview: e.tensor_scalar(
                    sb["ccol"][:, 4:4 + n],
                    mview[:, 0:n, 1:2].rearrange("p j o -> p (j o)"),
                    1.0, EPS, ALU.mult, ALU.add),
                 reads=["mv"], writes=["rstd"])
        b.op("act", lambda e, n=n: e.activation(sb["ccol"][:, 4:4 + n],
                                                sb["ccol"][:, 4:4 + n], AF.Ln),
             reads=["rstd"], writes=["rstd"])
        b.op("act", lambda e, n=n: e.activation(sb["ccol"][:, 4:4 + n],
                                                sb["ccol"][:, 4:4 + n],
                                                AF.Exp, scale=-0.5),
             reads=["rstd"], writes=["rstd"])

    def rope(ps_in, pskey, cos_ap, cos_keys, sin_ap, dst_ap, dstkey, n):
        b.op("dve", lambda e: e.tensor_copy(sb["rotb"][0:64, 0:n],
                                            ps_in[64:128, 0:n]),
             reads=[pskey], writes=["rotb"], inc=0)
        b.op("dve", lambda e: e.tensor_copy(sb["rotb"][64:128, 0:n],
                                            ps_in[0:64, 0:n]),
             reads=[pskey], writes=["rotb"])
        b.op("dve", lambda e: e.tensor_tensor(sb["tmpa"][:, 0:n], ps_in[:, 0:n],
                                              cos_ap, op=ALU.mult),
             reads=[pskey] + cos_keys, writes=["tmpa"])
        b.op("dve", lambda e: e.tensor_tensor(
                sb["tmpb"][:, 0:n], sb["rotb"][:, 0:n],
                sin_ap, op=ALU.mult),
             reads=["rotb"] + cos_keys, writes=["tmpb"])
        b.op("dve", lambda e: e.tensor_tensor(dst_ap, sb["tmpa"][:, 0:n],
                                              sb["tmpb"][:, 0:n], op=ALU.add),
             reads=["tmpa", "tmpb"], writes=[dstkey])

    def ln_apply(xa, xkey, xnt, j):
        b.op("dve", lambda e, j=j: e.tensor_scalar(
                xnt, xa, sb["ccol"][:, j:j + 1], sb["ccol"][:, 4 + j:5 + j],
                ALU.add, ALU.mult),
             reads=[xkey, "negmu", "rstd"], writes=["shr"])

    def transpose_group(gamma):
        for dc in range(16):
            for tt in range(4):
                xs = shr[:, tt * D + dc * 128: tt * D + dc * 128 + 128]
                b.op("pe", lambda e, xs=xs, tt=tt: e.matmul(
                        ps[0][:, tt * 128:(tt + 1) * 128], xs, sb["idn"][:],
                        start=True, stop=True),
                     reads=["shr", "idn"], writes=["ps0"],
                     inc=(1 if tt == 3 else 0))
            if gamma:
                b.op("dve", lambda e, dc=dc: e.tensor_scalar(
                        xnfm[:, dc, :], ps[0][:, :], sb["g16b"][:, dc:dc + 1],
                        None, ALU.mult, ALU.bypass),
                     reads=["ps0", "g16"], writes=["xnfm"])
            else:
                b.op("dve", lambda e, dc=dc: e.tensor_copy(
                        xnfm[:, dc, :], ps[0][:, :]),
                     reads=["ps0"], writes=["xnfm"])

    emit_quant_pipelined()
    b.barrier()
    csk = hv[:, 0:8192]
    snk = hv[:, 8192:16384]
    b.op("sp", lambda e: e.dma_start(out=csk, in_=dram["cosk"]),
         writes=["csk"], dma=True)
    b.op("sp", lambda e: e.dma_start(out=snk, in_=dram["sink"]),
         writes=["snk"], dma=True)
    wkv3 = hv[:, 16384:20480].rearrange("p (a c) -> p a c", c=256)
    b.op("pool", lambda e: e.dma_start(out=wkv3, in_=dram["wkv"]),
         writes=["wkv"], dma=True)

    def p2_loads(g):
        for tt in range(4):
            kt = 4 * g + tt
            xa = sb["xres"][:, tt * D:(tt + 1) * D]
            b.op("act", lambda e, kt=kt, xa=xa: e.dma_start(
                    out=xa, in_=dram["xkv"][kt * 128:(kt + 1) * 128, :]),
                 writes=["x%d" % tt], dma=True)

    p2_loads(0)
    for g in range(16):
        items = [("x%d" % tt, sb["xres"][:, tt * D:(tt + 1) * D])
                 for tt in range(4)]
        ln_stats4(items, double=True)
        for tt in range(4):
            ln_apply(items[tt][1], items[tt][0], shr[:, tt * D:(tt + 1) * D], tt)
        transpose_group(gamma=True)
        if g < 15:
            p2_loads(g + 1)
        for dc in range(16):
            b.op("pe", lambda e, dc=dc: e.matmul(
                    ps[1][:, :], wkv3[:, dc, 0:128], xnfm[:, dc, :],
                    start=(dc == 0), stop=(dc == 15)),
                 reads=["wkv", "xnfm"], writes=["ps1"], inc=(1 if dc == 15 else 0))
        rope(ps[1], "ps1", csk[:, g * 512:(g + 1) * 512], ["csk"],
             snk[:, g * 512:(g + 1) * 512],
             K[:, g * 512:(g + 1) * 512], "K", 512)
        for tt in range(4):
            kt = 4 * g + tt
            for dc in range(16):
                b.op("pe", lambda e, dc=dc, tt=tt: e.matmul(
                        ps[2][:, 0:128], xnfm[:, dc, tt * 128:(tt + 1) * 128],
                        wkv3[:, dc, 128:256], start=(dc == 0), stop=(dc == 15)),
                     reads=["xnfm", "wkv"], writes=["ps2"],
                     inc=(1 if dc == 15 else 0))
            b.op("dve", lambda e, kt=kt: e.tensor_copy(V[:, kt, 0:128],
                                                       ps[2][:, 0:128]),
                 reads=["ps2"], writes=["V"])

    for kb in range(16):
        b.op("act", lambda e, kb=kb: e.activation(
                sb["rotb"][:], K[:, kb * 512:(kb + 1) * 512], AF.Square),
             reads=["K"], writes=["rotb"])
        b.op("pe", lambda e: e.matmul(ps[7][0:1, 0:512], sb["onesc"][:],
                                      sb["rotb"][:], start=True, stop=True),
             reads=["rotb", "onesc"], writes=["ps7"])
        b.op("dve", lambda e, kb=kb: e.tensor_reduce(
                sb["knb"][0:1, kb:kb + 1], ps[7][0:1, 0:512], op=ALU.max, axis=AX.X),
             reads=["ps7"], writes=["knb"])
    b.op("dve", lambda e: e.tensor_reduce(
            sb["knb"][0:1, 16:17], sb["knb"][0:1, 0:16], op=ALU.max, axis=AX.X),
         reads=["knb"], writes=["k2m"])

    emit_quant(192)

    if debug:
        b.op("pool", lambda e: e.dma_start(out=dram["DBG_K"], in_=K[:]),
             reads=["K"], dma=True)
        b.op("pool", lambda e: e.dma_start(
                out=dram["DBG_V"], in_=V.ap().rearrange("p t c -> p (t c)")),
             reads=["V", "Vones"], dma=True)
    b.barrier()

    for s in range(NSC):
        b.op("sp", lambda e, s=s: e.dma_start(
                out=sb["cqb"][:], in_=dram["cosq"][:, s * 512:(s + 1) * 512]),
             writes=["cq"], dma=True)
        b.op("sp", lambda e, s=s: e.dma_start(
                out=sb["sqb"][:], in_=dram["sinq"][:, s * 512:(s + 1) * 512]),
             writes=["cq"], dma=True)

        items = []
        for tt in range(4):
            xa = sb["xres"][:, tt * D:(tt + 1) * D]
            b.op("act", lambda e, s=s, tt=tt, xa=xa: e.dma_start(
                    out=xa,
                    in_=dram["xq"][(4 * s + tt) * 128:(4 * s + tt + 1) * 128, :]),
                 writes=["x%d" % tt], dma=True)
            items.append(("x%d" % tt, xa))
        ln_stats4(items, double=True)
        for tt in range(4):
            ln_apply(items[tt][1], items[tt][0], shr[:, tt * D:(tt + 1) * D], tt)
        transpose_group(gamma=True)

        for h in range(NH):
            slot = h % 2
            wq = sb["wqs"][:, slot, :].rearrange("p (a c) -> p a c", c=128)
            b.op("pool", lambda e, h=h, wq=wq: e.dma_start(
                    out=wq, in_=dram["wq_t"][h]),
                 writes=["wq%d" % slot], dma=True)
            for dc in range(16):
                b.op("pe", lambda e, dc=dc, wq=wq: e.matmul(
                        ps[6][:, :], wq[:, dc, :], xnfm[:, dc, :],
                        start=(dc == 0), stop=(dc == 15)),
                     reads=["wq%d" % slot, "xnfm"], writes=["ps6"],
                     inc=(1 if dc == 15 else 0))
            rope(ps[6], "ps6", sb["cqb"][:], ["cq"], sb["sqb"][:],
                 Q[:, h, :], "shr", 512)
            b.op("act", lambda e, h=h: e.activation(
                    sb["rotb"][:], Q[:, h, :], AF.Square),
                 reads=["shr"], writes=["rotb"])
            b.op("pe", lambda e: e.matmul(ps[7][0:1, 0:512], sb["onesc"][:],
                                          sb["rotb"][:], start=True, stop=True),
                 reads=["rotb", "onesc"], writes=["ps7"])
            b.op("dve", lambda e: e.tensor_reduce(
                    sb["negsc"][0:1, 0:1], ps[7][0:1, 0:512], op=ALU.max, axis=AX.X),
                 reads=["ps7"], writes=["negsc"])
            b.op("dve", lambda e: e.tensor_scalar(
                    sb["negsc"][0:1, 1:2], sb["negsc"][0:1, 0:1],
                    sb["knb"][0:1, 16:17], None, ALU.mult, ALU.bypass),
                 reads=["negsc", "k2m"], writes=["negsc"])
            b.op("act", lambda e: e.activation(sb["negsc"][0:1, 1:2],
                                               sb["negsc"][0:1, 1:2], AF.Ln),
                 reads=["negsc"], writes=["negsc"])
            b.op("act", lambda e: e.activation(sb["negsc"][0:1, 1:2],
                                               sb["negsc"][0:1, 1:2],
                                               AF.Exp, scale=0.5),
                 reads=["negsc"], writes=["negsc"])
            b.op("dve", lambda e: e.tensor_copy(sb["negc1"][0:1, 0:1],
                                                sb["negsc"][0:1, 1:2]),
                 reads=["negsc"], writes=["negc1"])
            b.op("pe", lambda e: e.matmul(ps[7][:, 128:129], sb["onesr"][:],
                                          sb["negc1"][0:1, 0:1],
                                          start=True, stop=True),
                 reads=["negc1", "onesr"], writes=["ps7b"])
            b.op("dve", lambda e, h=h: e.tensor_scalar(
                    sb["cb16"][:, h:h + 1], ps[7][:, 128:129], -1.0, None,
                    ALU.mult, ALU.bypass),
                 reads=["ps7b"], writes=["cb16"])

        if debug and s == 0:
            b.op("pool", lambda e: e.dma_start(out=dram["DBG_Q"], in_=shr),
                 reads=["shr"], dma=True)

        for h in range(NH):
            nkb = 4 * s + 4
            iters = []
            for kb in range(nkb):
                j = max(0, kb - 4 * s)
                for kk in range(4):
                    iters.append((kb, kk, j))
            banks = [0, 1, 7]
            ptbs = [sb["PT0"], sb["PT1"], sb["PT2"]]

            def st_mm(i):
                kb, kk, j = iters[i]
                q0 = 128 * j
                tail = 512 - q0
                pp = banks[i % 3]
                pskey = "ps%d" % pp
                b.op("pe", lambda e, kb=kb, kk=kk, q0=q0, tail=tail, pp=pp, h=h:
                     e.matmul(ps[pp][:, 0:tail],
                              K[:, kb * 512 + kk * 128: kb * 512 + kk * 128 + 128],
                              Q[:, h, q0:512], start=True, stop=True),
                     reads=["K", "shr"], writes=[pskey])
                if kb >= 4 * s:
                    b.op("dve", lambda e, kk=kk, pp=pp: e.tensor_tensor(
                            ps[pp][:, 0:128], ps[pp][:, 0:128],
                            sb["mskB"][:, kk * 128:(kk + 1) * 128], op=ALU.add),
                         reads=[pskey, "mskB"], writes=[pskey])

            def exp_op(i):
                kb, kk, j = iters[i]
                tail = 512 - 128 * j
                pp = banks[i % 3]
                ptb = ptbs[i % 3]
                b.op("act", lambda e, ptb=ptb, tail=tail, pp=pp, h=h: e.activation(
                        ptb[:, 0:tail], ps[pp][:, 0:tail], AF.Exp,
                        bias=sb["cb16"][:, h:h + 1]),
                     reads=["ps%d" % pp, "cb16"], writes=["PT%d" % (i % 3)])

            def av_ops(i):
                kb, kk, j = iters[i]
                q0 = 128 * j
                ptb = ptbs[i % 3]
                ptkey = "PT%d" % (i % 3)
                for tq in range(j, 4):
                    t = 4 * s + tq
                    b.op("pe", lambda e, tq=tq, q0=q0, kb=kb, kk=kk, ptb=ptb:
                         e.matmul(ps[2 + tq][:, 0:129],
                                  ptb[:, tq * 128 - q0: tq * 128 - q0 + 128],
                                  V[:, kb * 4 + kk, :],
                                  start=(kb == 0 and kk == 0),
                                  stop=(kb == t and kk == 3)),
                         reads=[ptkey, "V", "Vones"], writes=["psO%d" % tq],
                         inc=(1 if (kb == t and kk == 3) else 0))

            n_it = len(iters)
            for i in range(n_it + 2):
                if i < n_it:
                    st_mm(i)
                if 1 <= i <= n_it:
                    exp_op(i - 1)
                if i >= 2:
                    av_ops(i - 2)
            for tq in range(4):
                b.op("dve", lambda e, tq=tq: e.reciprocal(
                        sb["ccol"][:, 9:10], ps[2 + tq][:, 128:129]),
                     reads=["psO%d" % tq], writes=["cc2"])
                b.op("dve", lambda e, tq=tq: e.tensor_scalar(
                        sb["rotb"][:, 0:128], ps[2 + tq][:, 0:128],
                        sb["ccol"][:, 9:10], None, ALU.mult, ALU.bypass),
                     reads=["psO%d" % tq, "cc2"], writes=["rotb"])
                b.op("pe", lambda e, tq=tq: e.matmul(
                        ps[6][:, tq * 128:(tq + 1) * 128],
                        sb["rotb"][:, 0:128], sb["idn"][:],
                        start=True, stop=True),
                     reads=["rotb", "idn"], writes=["ps6"],
                     inc=(1 if tq == 3 else 0))
            b.op("act", lambda e, h=h: e.activation(xnfm[:, h, :], ps[6][:, :],
                                                    AF.Copy),
                 reads=["ps6"], writes=["xnfm"])

        if debug and s == 0:
            b.op("pool", lambda e: e.dma_start(
                    out=dram["DBG_O"], in_=xnfm.ap().rearrange("p h c -> p (h c)")),
                 reads=["xnfm"], dma=True)

        for oc in range(16):
            slot = oc % 2
            wo = sb["wqs"][:, slot, :].rearrange("p (a c) -> p a c", c=128)
            b.op("pool", lambda e, oc=oc, wo=wo: e.dma_start(
                    out=wo, in_=dram["wo_t"][oc]),
                 writes=["wq%d" % slot], dma=True)
            for fc in range(16):
                b.op("pe", lambda e, fc=fc, wo=wo: e.matmul(
                        ps[0][:, :], wo[:, fc, :], xnfm[:, fc, :],
                        start=(fc == 0), stop=(fc == 15)),
                     reads=["wq%d" % slot, "xnfm"], writes=["ps0"],
                     inc=(1 if fc == 15 else 0))
            ev = sb["evac"][:, oc % 2, :]
            evk = "ev%d" % (oc % 2)
            b.op("act", lambda e, ev=ev: e.activation(ev, ps[0][:, :], AF.Copy),
                 reads=["ps0"], writes=[evk])
            for tt in range(4):
                b.op("pe", lambda e, ev=ev, tt=tt: e.matmul(
                        ps[1][:, tt * 128:(tt + 1) * 128],
                        ev[:, tt * 128:(tt + 1) * 128], sb["idn"][:],
                        start=True, stop=True),
                     reads=[evk, "idn"], writes=["ps1"], inc=(1 if tt == 3 else 0))
            dst = art[:, :, oc * 128:(oc + 1) * 128]
            b.op("dve", lambda e, dst=dst: e.tensor_copy(
                    dst, ps[1].ap().rearrange("p (t c) -> p t c", c=128)),
                 reads=["ps1"], writes=["shr"])
        if debug and s == 0:
            b.op("pool", lambda e: e.dma_start(out=dram["DBG_AR"], in_=shr),
                 reads=["shr"], dma=True)

        t2f = hvf[:, 0:2048]
        ln_stats4([("shr", art[:, tt, :]) for tt in range(4)], double=False)
        for tt in range(4):
            a_tt = art[:, tt, :]
            xa = sb["xres"][:, tt * D:(tt + 1) * D]
            b.op("dve", lambda e, a_tt=a_tt, t2f=t2f, tt=tt: e.tensor_scalar(
                    t2f, a_tt, sb["ccol"][:, tt:tt + 1], sb["ccol"][:, 4 + tt:5 + tt],
                    ALU.add, ALU.mult),
                 reads=["shr", "negmu", "rstd"], writes=["hs"])
            b.op("pool", lambda e, xa=xa, t2f=t2f: e.tensor_tensor(
                    xa, t2f, xa, op=ALU.add),
                 reads=["hs", "x%d" % tt], writes=["x%d" % tt])
        if debug and s == 0:
            b.op("sp", lambda e: e.dma_start(out=dram["DBG_X2"], in_=sb["xres"][:]),
                 reads=["x0", "x1", "x2", "x3"], dma=True)

        ln_stats4([("x%d" % tt, sb["xres"][:, tt * D:(tt + 1) * D])
                   for tt in range(4)], double=False)
        for tt in range(4):
            ln_apply(sb["xres"][:, tt * D:(tt + 1) * D], "x%d" % tt,
                     shr[:, tt * D:(tt + 1) * D], tt)
        transpose_group(gamma=False)

        for g in range(64):
            slot = g % 2
            gk, uk = "wgA%d" % slot, "wgB%d" % slot
            b.op("sp", lambda e, g=g, slot=slot: e.dma_start(
                    out=sb["wgub"][:, slot, :, :].rearrange("p a c -> p (a c)"),
                    in_=dram["sguT"][g]),
                 reads=["scr"], writes=[gk, "wp0"], dma=True)
            b.op("sp", lambda e, g=g, slot=slot: e.dma_start(
                    out=sb["wgu2"][:, slot, :, :].rearrange("p a c -> p (a c)"),
                    in_=dram["sguT"][64 + g]),
                 reads=["scr"], writes=[uk, "wp1"], dma=True)
            if debug and s == 0 and g == 7:
                b.op("pool", lambda e, slot=slot: e.dma_start(
                        out=dram["DBG_GS"],
                        in_=sb["wgub"][:, slot, :, :].rearrange("p a c -> p (a c)")),
                     reads=[gk], dma=True)
                b.op("pool", lambda e, slot=slot: e.dma_start(
                        out=dram["DBG_US"],
                        in_=sb["wgu2"][:, slot, :, :].rearrange("p a c -> p (a c)")),
                     reads=[uk], dma=True)
            pg = 0 if g % 2 == 0 else 2
            pu = 1 if g % 2 == 0 else 3
            for dc in range(16):
                b.op("pe", lambda e, slot=slot, dc=dc, pg=pg: e.matmul(
                        ps[pg][:, :], sb["wgub"][:, slot, dc, :], xnfm[:, dc, :],
                        start=(dc == 0), stop=(dc == 15)),
                     reads=[gk, "xnfm"], writes=["ps%d" % pg],
                     inc=(1 if dc == 15 else 0))
            for dc in range(16):
                b.op("pe", lambda e, slot=slot, dc=dc, pu=pu: e.matmul(
                        ps[pu][:, :], sb["wgu2"][:, slot, dc, :], xnfm[:, dc, :],
                        start=(dc == 0), stop=(dc == 15)),
                     reads=[uk, "xnfm"], writes=["ps%d" % pu],
                     inc=(1 if dc == 15 else 0))
            b.op("act", lambda e, pg=pg: e.activation(sb["sgx"][:], ps[pg][:, :],
                                                      AF.Silu),
                 reads=["ps%d" % pg], writes=["sgx"])
            b.op("dve", lambda e, pu=pu: e.tensor_copy(sb["sux"][:], ps[pu][:, :]),
                 reads=["ps%d" % pu], writes=["sux"])
            b.op("dve", lambda e, g=g: e.tensor_tensor(
                    sb["hslab"][:, g, :], sb["sgx"][:], sb["sux"][:], op=ALU.mult),
                 reads=["sgx", "sux"], writes=["hs"])
        if debug and s == 0:
            b.op("pool", lambda e: e.dma_start(
                    out=dram["DBG_H"], in_=hv[:, 0:2048]),
                 reads=["hs"], dma=True)

        for dc in range(16):
            for hh in range(2):
                wpt = sb["wgub"] if hh == 0 else sb["wgu2"]
                wpk = "wgA0" if hh == 0 else "wgB0"
                wp3 = wpt.ap().rearrange("p s a c -> p (s a) c")
                b.op("sp", lambda e, dc=dc, hh=hh, wp3=wp3: e.dma_start(
                        out=wp3.rearrange("p a c -> p (a c)"),
                        in_=dram["sprT"][dc][:, hh * 4096:(hh + 1) * 4096]),
                     reads=["scr"], writes=[wpk, "wgA1" if hh == 0 else "wgB1"],
                     dma=True)
                pf = 0 if dc % 2 == 0 else 2
                for hc in range(32):
                    b.op("pe", lambda e, hc=hc, hh=hh, wp3=wp3, dc=dc, pf=pf: e.matmul(
                            ps[pf][:, :], wp3[:, hc, :],
                            sb["hslab"][:, hh * 32 + hc, :],
                            start=(hh == 0 and hc == 0), stop=(hh == 1 and hc == 31)),
                         reads=[wpk, "hs"], writes=["ps%d" % pf],
                         inc=(1 if (hh == 1 and hc == 31) else 0))
            pf = 0 if dc % 2 == 0 else 2
            pt2 = 1 if dc % 2 == 0 else 3
            ev = sb["evac"][:, dc % 2, :]
            evk = "ev%d" % (dc % 2)
            b.op("act", lambda e, ev=ev, pf=pf: e.activation(ev, ps[pf][:, :],
                                                             AF.Copy),
                 reads=["ps%d" % pf], writes=[evk])
            for tt in range(4):
                b.op("pe", lambda e, ev=ev, tt=tt, pt2=pt2: e.matmul(
                        ps[pt2][:, tt * 128:(tt + 1) * 128],
                        ev[:, tt * 128:(tt + 1) * 128], sb["idn"][:],
                        start=True, stop=True),
                     reads=[evk, "idn"], writes=["ps%d" % pt2],
                     inc=(1 if tt == 3 else 0))
            dst = art[:, :, dc * 128:(dc + 1) * 128]
            b.op("dve", lambda e, dst=dst, pt2=pt2: e.tensor_copy(
                    dst, ps[pt2].ap().rearrange("p (t c) -> p t c", c=128)),
                 reads=["ps%d" % pt2], writes=["shr"])

        if debug and s == 0:
            b.op("pool", lambda e: e.dma_start(
                    out=dram["DBG_FFT"], in_=shr), reads=["shr"], dma=True)
            b.op("pool", lambda e: e.dma_start(
                    out=dram["DBG_HF"], in_=hv[:, 0:32768]), reads=["hs"], dma=True)
        ln_stats4([("shr", art[:, tt, :]) for tt in range(4)], double=False)
        for tt in range(4):
            a_tt = art[:, tt, :]
            xa = sb["xres"][:, tt * D:(tt + 1) * D]
            ob = hvf[:, 2048 + (tt % 2) * 2048: 2048 + (tt % 2 + 1) * 2048]
            obk = "ob%d" % (tt % 2)
            b.op("dve", lambda e, a_tt=a_tt, t2f=t2f, tt=tt: e.tensor_scalar(
                    t2f, a_tt, sb["ccol"][:, tt:tt + 1], sb["ccol"][:, 4 + tt:5 + tt],
                    ALU.add, ALU.mult),
                 reads=["shr", "negmu", "rstd"], writes=["hs"])
            b.op("pool", lambda e, xa=xa, ob=ob, t2f=t2f: e.tensor_tensor(
                    ob, t2f, xa, op=ALU.add),
                 reads=["hs", "x%d" % tt], writes=[obk])
            if debug and s == 0 and tt == 0:
                b.op("sp", lambda e, ob=ob: e.dma_start(out=dram["DBG_OB"], in_=ob),
                     reads=[obk, "hs"], dma=True)
                b.op("sp", lambda e, t2f=t2f: e.dma_start(out=dram["DBG_T2"], in_=t2f),
                     reads=["hs"], dma=True)
                b.op("sp", lambda e: e.dma_start(out=dram["DBG_CC"], in_=sb["ccol"][:]),
                     reads=["negmu", "rstd"], dma=True)
                b.op("pool", lambda e, a_tt=a_tt: e.dma_start(out=dram["DBG_ART0"], in_=a_tt),
                     reads=["shr"], dma=True)
            b.op("act", lambda e, s=s, tt=tt, ob=ob: e.dma_start(
                    out=dram["OUT"][(4 * s + tt) * 128:(4 * s + tt + 1) * 128, :],
                    in_=ob),
                 reads=[obk, "hs"], dma=True)

    if debug:
        b.op("pool", lambda e: e.dma_start(out=dram["DBG_WH"],
                                           in_=dram["sguT"][7].rearrange("p c -> p c")),
             reads=["scr"], dma=True)
    b.barrier()


def _rope_tables():
    inv = 1.0 / 10000.0 ** (np.arange(0, DH, 2, dtype=np.float64) / DH)
    pos = np.arange(SEQ, dtype=np.float64)
    fr = pos[:, None] * inv[None, :]
    fr = np.concatenate([fr, fr], axis=1)
    cos = np.cos(fr).astype(np.float32)
    sin = np.sin(fr).astype(np.float32)
    sgn = np.ones((1, DH), np.float32)
    sgn[0, :64] = -1.0
    return cos.T.copy(), (sin * sgn).T.copy()


def kernel(L0_attn_x, L0_attn_gamma, L0_attn_Wq, L0_attn_Wkv, L0_attn_Wo,
           L0_ffn_w_gate_up, L0_ffn_w_proj):
    x = np.ascontiguousarray(np.asarray(L0_attn_x, dtype=np.float32))
    gamma = np.asarray(L0_attn_gamma, dtype=np.float32)
    Wq = np.asarray(L0_attn_Wq, dtype=np.float32)
    Wkv = np.asarray(L0_attn_Wkv, dtype=np.float32)
    Wo = np.asarray(L0_attn_Wo, dtype=np.float32)
    Wgu = np.ascontiguousarray(np.asarray(L0_ffn_w_gate_up, dtype=np.float32))
    Wpr = np.ascontiguousarray(np.asarray(L0_ffn_w_proj, dtype=np.float32))

    cosk, sink = _rope_tables()
    scale = DH ** -0.5
    wq_t = np.ascontiguousarray(Wq.reshape(16, 128, NH, 128).transpose(2, 1, 0, 3))
    wo_t = np.ascontiguousarray(Wo.reshape(16, 128, 16, 128).transpose(2, 1, 0, 3))
    wkv_t = np.ascontiguousarray(Wkv.reshape(16, 128, 256).transpose(1, 0, 2))
    g16 = np.ascontiguousarray(gamma.reshape(16, 128).T)
    ident = np.eye(128, dtype=np.float32).astype(bf16)
    cosk_b = cosk.astype(bf16)
    sink_b = sink.astype(bf16)

    in_maps = []
    for c in range(8):
        bb, i = divmod(c, 4)
        idx = [4 * t + i for t in range(NQT)]
        xq = np.ascontiguousarray(x[bb].reshape(64, 128, D)[idx].reshape(TOK, D))
        cq = np.ascontiguousarray(
            (cosk * scale).reshape(128, 64, 128)[:, idx, :].reshape(128, TOK)
        ).astype(bf16)
        sq = np.ascontiguousarray(
            (sink * scale).reshape(128, 64, 128)[:, idx, :].reshape(128, TOK)
        ).astype(bf16)
        r = np.arange(128)[:, None]
        cc = np.arange(128)[None, :]
        mB = np.zeros((128, 512), dtype=np.float32)
        for kk in range(4):
            mB[:, kk * 128:(kk + 1) * 128] = np.where(
                (128 * kk + r) <= (128 * i + cc), 0.0, NEG)
        in_maps.append(dict(
            xq=xq, xkv=np.ascontiguousarray(x[bb]), wq_t=wq_t, wo_t=wo_t,
            wkv=wkv_t, wgu=Wgu, wpr=Wpr, g16=g16,
            cosk=cosk_b, sink=sink_b, cosq=cq, sinq=sq,
            maskB=mB.astype(bf16), ident=ident))

    nc = build_nc(debug=DBG)
    res = run_bass_kernel_spmd(nc, in_maps, core_ids=list(range(8)))

    out = np.empty((B_, SEQ, D), dtype=np.float32)
    for c in range(8):
        bb, i = divmod(c, 4)
        rr = np.asarray(res.results[c]["OUT"], dtype=np.float32)
        idx = [4 * t + i for t in range(NQT)]
        out[bb].reshape(64, 128, D)[idx] = rr.reshape(NQT, 128, D)
    return out


